# version 06
# baseline (speedup 1.0000x reference)
import math
import numpy as np
import concourse.bass as bass
import concourse.mybir as mybir
from concourse.bass_utils import run_bass_kernel_spmd
from contextlib import ExitStack

F32 = mybir.dt.float32
BF16 = mybir.dt.bfloat16
I32 = mybir.dt.int32
AF = mybir.ActivationFunctionType
ALU = mybir.AluOpType

B, T, D = 2, 4096, 2048
H, DH = 16, 128
HPC = 4
NB = 8
TPB = 4
NKC = 32
SCALE = 1.0 / math.sqrt(DH)
ROWS = 1024
TWO_PI = 2.0 * math.pi
EPS = 1e-5
ENGINES = ("sync", "tensor", "vector", "scalar", "gpsimd")


class Emitter:
    def __init__(self, nc, ctx):
        self.nc = nc
        self.q = {e: [] for e in ENGINES}
        self.sem = {n: ctx.enter_context(nc.semaphore(n))
                    for n in ("ld", "st", "gd", "cc", "te", "ve", "sc", "gp")}
        self.cnt = {n: 0 for n in self.sem}

    def op(self, eng, fn, inc=None, amt=None):
        if inc is not None:
            if amt is None:
                amt = 16 if inc in ("ld", "st", "gd") else 1
            self.cnt[inc] += amt
            mark = (inc, self.cnt[inc])
        else:
            mark = None
        self.q[eng].append(("op", fn, inc, amt))
        return mark

    def wait(self, eng, sem, val=None):
        v = self.cnt[sem] if val is None else val
        if v <= 0:
            return
        self.q[eng].append(("wait", sem, v, None))

    def wait_mark(self, eng, mark):
        if mark is not None:
            self.wait(eng, mark[0], mark[1])

    def emit(self, block):
        for ename in ENGINES:
            items = self.q[ename]
            if not items:
                continue
            section = getattr(block, ename)

            def body(e, items=items):
                for it in items:
                    if it[0] == "op":
                        _, fn, inc, amt = it
                        instr = fn(e)
                        if inc is not None:
                            instr.then_inc(self.sem[inc], amt)
                    else:
                        _, sname, v, _ = it
                        e.wait_ge(self.sem[sname], v)
            section(body)


def build_nc(debug=False):
    nc = bass.Bass("TRN2", target_bir_lowering=False)

    x_in = nc.dram_tensor("x_in", [T, D], F32, kind="ExternalInput").ap()
    wqkv_in = nc.dram_tensor("wqkv_in", [128, 16 * 1536], F32, kind="ExternalInput").ap()
    bias_col_in = nc.dram_tensor("bias_col_in", [128, 12], F32, kind="ExternalInput").ap()
    bias_swap_in = nc.dram_tensor("bias_swap_in", [128, 8], F32, kind="ExternalInput").ap()
    bias_vrow_in = nc.dram_tensor("bias_vrow_in", [1, 512], F32, kind="ExternalInput").ap()
    wout_in = nc.dram_tensor("wout_in", [D, D], F32, kind="ExternalInput").ap()
    fac_in = nc.dram_tensor("fac_in", [8, 256, D], F32, kind="ExternalInput").ap()
    bout_in = nc.dram_tensor("bout_in", [1, D], F32, kind="ExternalInput").ap()
    fbias_in = nc.dram_tensor("fbias_in", [1, D], F32, kind="ExternalInput").ap()
    y_out = nc.dram_tensor("y_out", [ROWS, D], F32, kind="ExternalOutput").ap()

    xnT_dram = nc.dram_tensor("xnT_dram", [D, T], BF16).ap()
    tabs_dram = nc.dram_tensor("tabs_dram", [2, 128, T], F32).ap()
    agF_in = nc.dram_tensor("agF_in", [256, D], BF16).ap()
    agF_out = nc.dram_tensor("agF_out", [D, D], BF16).ap()
    agA_in = nc.dram_tensor("agA_in", [4, 512, 1024], BF16).ap()
    agA_out = nc.dram_tensor("agA_out", [4, 8, 512, 1024], BF16).ap()

    dbg = {}
    if debug:
        dbg["xnT"] = nc.dram_tensor("dbg_xnT", [D, T], BF16, kind="ExternalOutput").ap()
        dbg["agA"] = nc.dram_tensor("dbg_agA", [4, 512, 1024], BF16, kind="ExternalOutput").ap()
        dbg["agF"] = nc.dram_tensor("dbg_agF", [D, D], BF16, kind="ExternalOutput").ap()
        dbg["kT"] = nc.dram_tensor("dbg_kT", [128, T], BF16, kind="ExternalOutput").ap()
        dbg["v"] = nc.dram_tensor("dbg_v", [128, NKC * 512], BF16, kind="ExternalOutput").ap()
        dbg["qT"] = nc.dram_tensor("dbg_qT", [128, HPC * 512], BF16, kind="ExternalOutput").ap()

    ctx = ExitStack()
    em = Emitter(nc, ctx)

    psA = ctx.enter_context(nc.psum_tensor("psA", [128, 1024], F32))
    psB = ctx.enter_context(nc.psum_tensor("psB", [128, 1024], F32))
    ps_pv = ctx.enter_context(nc.psum_tensor("ps_pv", [128, 512], F32))
    ps_sum = ctx.enter_context(nc.psum_tensor("ps_sum", [1, 512], F32))
    ps_v = ctx.enter_context(nc.psum_tensor("ps_v", [128, 512], F32))
    ps_bc = ctx.enter_context(nc.psum_tensor("ps_bc", [128, 512], F32))

    ones_col_bf = ctx.enter_context(nc.sbuf_tensor("ones_col_bf", [128, 1], BF16))
    ones_row_bf = ctx.enter_context(nc.sbuf_tensor("ones_row_bf", [1, 128], BF16))
    ones_row_f = ctx.enter_context(nc.sbuf_tensor("ones_row_f", [1, 128], F32))
    ident_bf = ctx.enter_context(nc.sbuf_tensor("ident_bf", [128, 128], BF16))
    ones128_bf = ctx.enter_context(nc.sbuf_tensor("ones128_bf", [128, 128], BF16))
    bias_col = ctx.enter_context(nc.sbuf_tensor("bias_col", [128, 12], F32))
    bias_swap = ctx.enter_context(nc.sbuf_tensor("bias_swap", [128, 8], F32))
    bias_v_bf = ctx.enter_context(nc.sbuf_tensor("bias_v_bf", [1, 512], BF16))
    bias2_bf = ctx.enter_context(nc.sbuf_tensor("bias2_bf", [1, D], BF16))
    row_a = ctx.enter_context(nc.sbuf_tensor("row_a", [1, 1024], F32))
    row_c = ctx.enter_context(nc.sbuf_tensor("row_c", [1, 1024], F32))
    frq_a = ctx.enter_context(nc.sbuf_tensor("frq_a", [1, 128], F32))
    frq_b = ctx.enter_context(nc.sbuf_tensor("frq_b", [1, 128], F32))
    iota_blk = ctx.enter_context(nc.sbuf_tensor("iota_blk", [1, 512], F32))
    stat = ctx.enter_context(nc.sbuf_tensor("stat", [128, 8], F32))

    ctx12 = ExitStack()
    wq_s = ctx12.enter_context(nc.sbuf_tensor("wq_s", [128, 16 * 1536], BF16))
    kT = [ctx12.enter_context(nc.sbuf_tensor(f"kT{h}", [128, T], BF16)) for h in range(HPC)]
    v_s = ctx12.enter_context(nc.sbuf_tensor("v_s", [128, NKC * 512], BF16))
    xstage = ctx12.enter_context(nc.sbuf_tensor("xstage", [128, 2 * D], F32))
    qT = [ctx12.enter_context(nc.sbuf_tensor(f"qT{h}", [128, 512], BF16)) for h in range(HPC)]
    rs_a = ctx12.enter_context(nc.sbuf_tensor("rs_a", [128, 512], F32))
    rs_b = ctx12.enter_context(nc.sbuf_tensor("rs_b", [128, 512], F32))
    expS = ctx12.enter_context(nc.sbuf_tensor("expS", [128, 3 * 1024], BF16))
    att_o = ctx12.enter_context(nc.sbuf_tensor("att_o", [128, 2 * 512], BF16))
    att_u = ctx12.enter_context(nc.sbuf_tensor("att_u", [128, 2 * 512], F32))
    rbc = ctx12.enter_context(nc.sbuf_tensor("rbc", [128, 512], F32))
    ln_sq = ctx12.enter_context(nc.sbuf_tensor("ln_sq", [128, D], BF16))
    cos_blk = ctx12.enter_context(nc.sbuf_tensor("cos_blk", [128, 512], F32))
    sin_blk = ctx12.enter_context(nc.sbuf_tensor("sin_blk", [128, 512], F32))
    ctxF = ExitStack()
    facc = ctxF.enter_context(nc.sbuf_tensor("facc", [128, 2 * D], F32))
    fshard = ctxF.enter_context(nc.sbuf_tensor("fshard", [128, 2 * D], BF16))
    ctxF.close()
    xn_tm = ctx12.enter_context(nc.sbuf_tensor("xn_tm", [128, TPB * D], BF16))
    xnT_j = ctx12.enter_context(nc.sbuf_tensor("xnT_j", [128, 16 * 512], BF16))

    em.op("sync", lambda e: e.dma_start(out=bias_col[:, :], in_=bias_col_in), "ld")
    em.op("sync", lambda e: e.dma_start(out=bias_swap[:, :], in_=bias_swap_in), "ld")
    mk_bv = em.op("sync", lambda e: e.dma_start(out=row_a[0:1, 0:512], in_=bias_vrow_in), "ld")
    mk_bo = em.op("sync", lambda e: e.dma_start(out=xstage[0:1, 0:D], in_=bout_in), "ld")
    em.op("gpsimd", lambda e: e.memset(ones_col_bf[:, :], 1.0), "gp")
    em.op("gpsimd", lambda e: e.memset(ones_row_bf[:, :], 1.0), "gp")
    em.op("gpsimd", lambda e: e.memset(ones_row_f[:, :], 1.0), "gp")
    em.op("gpsimd", lambda e: e.memset(ones128_bf[:, :], 1.0), "gp")
    em.op("gpsimd", lambda e: e.memset(stat[:, 7:8], EPS), "gp")
    mk_id = em.op("gpsimd", lambda e: e.affine_select(
        ident_bf[:, :], ones128_bf[:, :], pattern=[[1, 128]], compare_op=ALU.is_equal,
        fill=0.0, base=0, channel_multiplier=-1), "gp")

    mk_wq = None
    for s in range(8):
        mk_wq = em.op("gpsimd", lambda e, s=s: e.dma_start(
            out=wq_s[:, s * 3072:(s + 1) * 3072],
            in_=wqkv_in[:, s * 3072:(s + 1) * 3072]), "gd")

    em.wait_mark("gpsimd", mk_bo)
    mk_b2 = em.op("gpsimd", lambda e: e.dma_start(out=xstage[0:1, 0:D], in_=fbias_in,
                                                  accum_op=ALU.add), "gd")
    em.wait_mark("vector", mk_b2)
    mk_b2c = em.op("vector", lambda e: e.tensor_copy(bias2_bf[:, :], xstage[0:1, 0:D]), "ve")
    em.wait_mark("vector", mk_bv)
    mk_bvc = em.op("vector", lambda e: e.tensor_copy(bias_v_bf[:, :], row_a[0:1, 0:512]), "ve")

    fmk = []
    for r in range(8):
        if r > 0:
            em.wait("gpsimd", "gd")
        fmk.append(em.op("gpsimd", lambda e, r=r: e.dma_start(
            out=facc.ap().rearrange("p (c d) -> p c d", c=2),
            in_=fac_in[r].rearrange("(c p) d -> p c d", p=128),
            accum_op=(ALU.bypass if r == 0 else ALU.add)), "gd"))
    em.wait_mark("vector", fmk[-1])
    mk_fsc = em.op("vector", lambda e: e.tensor_scalar(
        fshard[:, :], facc[:, :], 1.0 / math.sqrt(8.0), None, ALU.mult), "ve")
    em.wait_mark("sync", mk_fsc)
    mk_fst = em.op("sync", lambda e: e.dma_start(
        out=agF_in.rearrange("(c p) d -> p c d", p=128),
        in_=fshard.ap().rearrange("p (c d) -> p c d", c=2)), "st")
    em.wait_mark("gpsimd", mk_fst)
    em.op("gpsimd", lambda e: e.collective_compute(
        "AllGather", ALU.bypass, replica_groups=[list(range(8))],
        ins=[agF_in.opt()], outs=[agF_out.opt()]), "cc")
    cc_F = em.cnt["cc"]

    em.op("gpsimd", lambda e: e.iota(frq_a[:, :], pattern=[[1, 128]], base=0,
                                     channel_multiplier=0,
                                     allow_small_or_imprecise_dtypes=True), "gp")
    em.wait("vector", "gp")
    em.op("vector", lambda e: e.tensor_scalar(frq_b[:, :], frq_a[:, :], 64.0, -64.0,
                                              ALU.is_ge, ALU.mult), "ve")
    em.op("vector", lambda e: e.drain(), None)
    mk_fq = em.op("vector", lambda e: e.tensor_add(frq_a[:, :], frq_a[:, :], frq_b[:, :]), "ve")
    em.wait_mark("scalar", mk_fq)
    mk_fe = em.op("scalar", lambda e: e.activation(frq_b[:, :], frq_a[:, :], AF.Exp,
                                                   scale=-math.log(10000.0) / 64.0), "sc")
    mk_tabs_done = None
    prev_r = None
    prev_th = None
    for ch in range(8):
        em.wait_mark("gpsimd", prev_th)
        mk_it = em.op("gpsimd", lambda e, ch=ch: e.iota(
            iota_blk[:, :], pattern=[[1, 512]], base=512 * ch,
            channel_multiplier=0, allow_small_or_imprecise_dtypes=True), "gp")
        em.wait_mark("tensor", mk_fe)
        em.wait_mark("tensor", mk_it)
        em.wait_mark("tensor", prev_r)
        mk_th = em.op("tensor", lambda e: e.matmul(
            ps_v[:, :], frq_b[:, :], iota_blk[:, :], start=True, stop=True), "te")
        prev_th = mk_th
        em.wait_mark("vector", mk_th)
        em.op("vector", lambda e: e.tensor_scalar(rs_a[:, :], ps_v[:, :], 1.0 / TWO_PI,
                                                  None, ALU.mult), "ve")
        em.op("vector", lambda e: e.tensor_copy(rs_b.ap().bitcast(I32), rs_a[:, :]), "ve")
        em.op("vector", lambda e: e.tensor_copy(rs_a[:, :], rs_b.ap().bitcast(I32)), "ve")
        mk_r = em.op("vector", lambda e: e.scalar_tensor_tensor(
            rs_b[:, :], rs_a[:, :], -TWO_PI, ps_v[:, :], ALU.mult, ALU.add), "ve")
        prev_r = mk_r
        em.op("vector", lambda e: e.tensor_scalar(rs_a[:, :], rs_b[:, :], math.pi / 2,
                                                  None, ALU.add), "ve")
        em.op("vector", lambda e: e.tensor_scalar(rbc[:, :], rs_a[:, :], math.pi,
                                                  -TWO_PI, ALU.is_gt, ALU.mult), "ve")
        mk_r2 = em.op("vector", lambda e: e.tensor_add(rs_a[:, :], rs_a[:, :], rbc[:, :]), "ve")
        em.wait_mark("scalar", mk_r2)
        mk_sin = em.op("scalar", lambda e: e.activation(rbc[:, :], rs_b[:, :], AF.Sin), "sc")
        mk_cos = em.op("scalar", lambda e: e.activation(rs_b[:, :], rs_a[:, :], AF.Sin), "sc")
        em.wait_mark("sync", mk_sin)
        em.op("sync", lambda e, ch=ch: e.dma_start(
            out=tabs_dram[1, :, ch * 512:(ch + 1) * 512], in_=rbc[:, :]), "st")
        em.wait_mark("sync", mk_cos)
        mk_tabs_done = em.op("sync", lambda e, ch=ch: e.dma_start(
            out=tabs_dram[0, :, ch * 512:(ch + 1) * 512], in_=rs_b[:, :]), "st")
        em.wait_mark("vector", mk_tabs_done)

    def psA_bfv():
        return psA.ap().bitcast(BF16)

    S = {
        "free_pv": None, "free_sum": None, "free_v": prev_r, "free_bc": None,
        "free_expS": [None, None, None], "free_atto": [None, None],
        "free_xstage": [mk_b2c, None], "free_stat": None, "prev_rstd": None,
        "free_cos": None, "free_rowa": None, "xpose_done": None,
        "mk_xnT_store": None, "readersA": {"gp": mk_id}, "readersB": {"gp": mk_id},
        "qT_free": [None] * HPC, "rope_mark": [None] * HPC,
        "cos_ready": None, "sin_ready": None, "pending_norm": [],
        "pending_ln": [], "pending_lnb": [], "row_a_free": [None, None],
        "last_scores_te": None,
    }
    vev_mark = {}

    def emit_ln_front(j, tt):
        sl = tt % 2
        em.wait_mark("sync", S["free_xstage"][sl])
        mk_x = em.op("sync", lambda e, j=j, tt=tt, sl=sl: e.dma_start(
            out=xstage[:, sl * D:(sl + 1) * D],
            in_=x_in[j * 512 + tt * 128: j * 512 + (tt + 1) * 128, :]), "ld")
        em.wait_mark("vector", mk_x)
        if j == 0 and tt == 0:
            em.wait_mark("vector", mk_fst)
        em.wait_mark("vector", S["xpose_done"])
        em.wait_mark("vector", S["prev_rstd"])
        mk_s1 = em.op("vector", lambda e, tt=tt, sl=sl: e.tensor_scalar(
            xn_tm[:, tt * D:(tt + 1) * D], xstage[:, sl * D:(sl + 1) * D],
            1.0, 0.0, ALU.mult, ALU.add, accum_out=stat[:, 0:1]), "ve")
        mk_s2 = em.op("vector", lambda e, sl=sl: e.scalar_tensor_tensor(
            ln_sq[:, :], xstage[:, sl * D:(sl + 1) * D], 0.0,
            xstage[:, sl * D:(sl + 1) * D], ALU.add, ALU.mult,
            accum_out=stat[:, 1:2]), "ve")
        S["pending_lnb"].append((j, tt, sl, mk_s2))

    def emit_ln_back():
        (j, tt, sl, mk_s2) = S["pending_lnb"].pop(0)
        em.wait_mark("gpsimd", mk_s2)
        em.wait_mark("gpsimd", S["free_stat"])
        if j == 0 and tt == 0:
            em.wait_mark("gpsimd", mk_id)
        em.op("gpsimd", lambda e: e.tensor_scalar(stat[:, 3:4], stat[:, 0:1],
                                                  -1.0 / D, 0.0, ALU.mult, ALU.add), "gp")
        em.op("gpsimd", lambda e: e.tensor_scalar(stat[:, 4:5], stat[:, 1:2],
                                                  1.0 / D, EPS, ALU.mult, ALU.add), "gp")
        em.op("gpsimd", lambda e: e.tensor_tensor(stat[:, 6:7], stat[:, 3:4],
                                                  stat[:, 3:4], ALU.mult), "gp")
        mk_var = em.op("gpsimd", lambda e: e.tensor_tensor(stat[:, 5:6], stat[:, 4:5],
                                                           stat[:, 6:7], ALU.subtract), "gp")
        em.wait_mark("scalar", mk_var)
        em.op("scalar", lambda e: e.activation(stat[:, 5:6], stat[:, 5:6], AF.Ln), "sc")
        em.op("scalar", lambda e: e.activation(stat[:, 5:6], stat[:, 5:6],
                                               AF.Exp, scale=-0.5), "sc")
        mk_rstd = em.op("scalar", lambda e: e.copy(stat[:, 2:3], stat[:, 5:6]), "sc")
        S["prev_rstd"] = mk_rstd
        em.wait_mark("vector", mk_rstd)
        mk_norm = em.op("vector", lambda e, tt=tt, sl=sl: e.tensor_scalar(
            xn_tm[:, tt * D:(tt + 1) * D], xstage[:, sl * D:(sl + 1) * D],
            stat[:, 3:4], stat[:, 5:6], ALU.add, ALU.mult), "ve")
        S["free_xstage"][sl] = mk_norm
        S["free_stat"] = mk_norm
        return mk_norm

    def emit_ln_tile(j, tt):
        emit_ln_front(j, tt)
        return emit_ln_back()

    def emit_xq(j, norm_last):
        em.wait_mark("sync", mk_tabs_done)
        em.wait_mark("sync", S["free_cos"])
        S["cos_ready"] = em.op("sync", lambda e, j=j: e.dma_start(
            out=cos_blk[:, :], in_=tabs_dram[0, :, j * 512:(j + 1) * 512]), "ld")
        S["sin_ready"] = em.op("sync", lambda e, j=j: e.dma_start(
            out=sin_blk[:, :], in_=tabs_dram[1, :, j * 512:(j + 1) * 512]), "ld")
        em.wait_mark("tensor", norm_last)
        if S["mk_xnT_store"] is not None:
            em.wait_mark("tensor", S["mk_xnT_store"])
        for r in S["readersA"].values():
            em.wait_mark("tensor", r)
        S["readersA"] = {}
        ev_marks = []
        mk_tp_last = None
        for dc in range(16):
            half = dc % 2
            if dc >= 2:
                em.wait_mark("tensor", ev_marks[dc - 2])
            for tt in range(TPB):
                mk_tp = em.op("tensor", lambda e, dc=dc, tt=tt, half=half: e.transpose(
                    psA_bfv()[:, half * 1024 + tt * 128: half * 1024 + (tt + 1) * 128],
                    xn_tm[:, tt * D + dc * 128: tt * D + (dc + 1) * 128],
                    ident_bf[:, :]), "te" if tt == 3 else None)
            mk_tp_last = mk_tp
            em.wait_mark("vector", mk_tp)
            ev = em.op("vector", lambda e, dc=dc, half=half: e.tensor_copy(
                xnT_j[:, dc * 512:(dc + 1) * 512],
                psA_bfv()[:, half * 1024: half * 1024 + 512]), "ve")
            ev_marks.append(ev)
        S["xpose_done"] = mk_tp_last
        S["readersA"]["ve"] = ev_marks[-1]
        em.wait_mark("sync", ev_marks[-1])
        S["mk_xnT_store"] = em.op("sync", lambda e, j=j: e.dma_start(
            out=xnT_dram.rearrange("(dc p) t -> p dc t", p=128)[:, :, j * 512:(j + 1) * 512],
            in_=xnT_j.ap().rearrange("p (dc t) -> p dc t", dc=16)), "st")

        em.wait_mark("tensor", mk_wq)
        slot_free = [None, None]
        for cc in range(8):
            bk = cc % 2
            tensorX = psA if bk == 0 else psB
            readers = S["readersA"] if bk == 0 else S["readersB"]
            for r in readers.values():
                em.wait_mark("tensor", r)
            readers.clear()
            em.wait_mark("tensor", slot_free[bk])
            mk_mm = None
            for dc in range(16):
                mk_mm = em.op("tensor", lambda e, cc=cc, dc=dc, tensorX=tensorX: e.matmul(
                    tensorX[:, 0:512],
                    wq_s[:, dc * 1536 + cc * 128: dc * 1536 + (cc + 1) * 128],
                    xnT_j[:, dc * 512:(dc + 1) * 512],
                    start=(dc == 0), stop=(dc == 15)), "te" if dc == 15 else None)
            is_q = cc < 4
            h = cc % 4
            dest = qT[h] if is_q else kT[h]
            dst_off = 0 if is_q else j * 512
            em.wait_mark("vector", mk_mm)
            em.wait_mark("vector", S["cos_ready"])
            em.wait_mark("vector", S["sin_ready"])
            if cc == 0 and j == 0:
                em.wait_mark("vector", mk_tabs_done)
            if is_q:
                em.wait_mark("vector", S["qT_free"][h])
            pb = tensorX
            em.op("vector", lambda e, pb=pb, cc=cc: e.scalar_tensor_tensor(
                rs_a[0:64, :], pb[0:64, 0:512], bias_col[0:64, cc:cc + 1],
                cos_blk[0:64, :], ALU.add, ALU.mult), "ve")
            em.op("vector", lambda e, pb=pb, cc=cc: e.scalar_tensor_tensor(
                rs_b[0:64, :], pb[64:128, 0:512], bias_swap[0:64, cc:cc + 1],
                sin_blk[0:64, :], ALU.add, ALU.mult), "ve")
            em.op("vector", lambda e, dest=dest, dst_off=dst_off: e.tensor_sub(
                dest[0:64, dst_off:dst_off + 512], rs_a[0:64, :], rs_b[0:64, :]), "ve")
            em.op("vector", lambda e, pb=pb, cc=cc: e.scalar_tensor_tensor(
                rs_a[64:128, :], pb[0:64, 0:512], bias_swap[64:128, cc:cc + 1],
                sin_blk[64:128, :], ALU.add, ALU.mult), "ve")
            em.op("vector", lambda e, pb=pb, cc=cc: e.scalar_tensor_tensor(
                rs_b[64:128, :], pb[64:128, 0:512], bias_col[64:128, cc:cc + 1],
                cos_blk[64:128, :], ALU.add, ALU.mult), "ve")
            mk_rope = em.op("vector", lambda e, dest=dest, dst_off=dst_off: e.tensor_add(
                dest[64:128, dst_off:dst_off + 512], rs_a[64:128, :], rs_b[64:128, :]), "ve")
            slot_free[bk] = mk_rope
            readers["ve"] = mk_rope
            S["rope_mark"][h] = mk_rope
        S["free_cos"] = S["rope_mark"][3]
        for tt in range(TPB):
            em.wait_mark("tensor", S["free_v"])
            for dc in range(16):
                em.op("tensor", lambda e, tt=tt, dc=dc: e.matmul(
                    ps_v[:, :],
                    xnT_j[:, dc * 512 + tt * 128: dc * 512 + (tt + 1) * 128],
                    wq_s[:, dc * 1536 + 1024: dc * 1536 + 1536],
                    start=(dc == 0), stop=False), None)
            mk_mm = em.op("tensor", lambda e: e.matmul(
                ps_v[:, :], ones_row_bf[:, :], bias_v_bf[:, :],
                start=False, stop=True), "te")
            em.wait_mark("vector", mk_mm)
            if j == 0 and tt == 0:
                em.wait_mark("vector", mk_bvc)
            g = 4 * j + tt
            mk_vev = em.op("vector", lambda e, g=g: e.tensor_copy(
                v_s[:, g * 512:(g + 1) * 512], ps_v[:, :]), "ve")
            S["free_v"] = mk_vev
            vev_mark[g] = mk_vev

    def emit_pv_pair(j, h, pi, slot, mk_ready, nkc):
        em.wait_mark("tensor", mk_ready)
        if pi == 0:
            em.wait_mark("tensor", S["free_pv"])
            em.wait_mark("tensor", S["free_sum"])
            em.wait_mark("tensor", vev_mark[nkc - 1])
        mk = None
        for half in (0, 1):
            kc = 2 * pi + half
            em.op("tensor", lambda e, h=h, kc=kc, slot=slot, half=half, nkc=nkc: e.matmul(
                ps_pv[:, :], v_s[:, kc * 512 + h * 128: kc * 512 + (h + 1) * 128],
                expS[:, slot * 1024 + half * 512: slot * 1024 + (half + 1) * 512],
                start=(kc == 0), stop=(kc == nkc - 1)), None)
            mk = em.op("tensor", lambda e, kc=kc, slot=slot, half=half, nkc=nkc: e.matmul(
                ps_sum[0:1, :], ones_col_bf[:, 0:1],
                expS[:, slot * 1024 + half * 512: slot * 1024 + (half + 1) * 512],
                start=(kc == 0), stop=(kc == nkc - 1)), "te")
        S["free_expS"][slot] = mk
        return mk

    def flush_norm(limit):
        while len(S["pending_norm"]) > limit:
            (j, h, mk_cs, uslot, aslot, cslot) = S["pending_norm"].pop(0)
            em.wait_mark("scalar", mk_cs)
            em.op("scalar", lambda e, cslot=cslot: e.activation(
                row_a[0:1, cslot * 512:(cslot + 1) * 512],
                row_a[0:1, cslot * 512:(cslot + 1) * 512], AF.Ln), "sc")
            em.wait_mark("scalar", S["row_c_free"][cslot])
            mk_rec = em.op("scalar", lambda e, cslot=cslot: e.activation(
                row_c[0:1, cslot * 512:(cslot + 1) * 512],
                row_a[0:1, cslot * 512:(cslot + 1) * 512],
                AF.Exp, scale=-1.0), "sc")
            S["row_a_free"][cslot] = mk_rec
            em.wait_mark("tensor", mk_rec)
            em.wait_mark("tensor", S["free_bc"])
            mk_bcm = em.op("tensor", lambda e, cslot=cslot: e.matmul(
                ps_bc[:, :], ones_row_f[:, :],
                row_c[0:1, cslot * 512:(cslot + 1) * 512], start=True,
                stop=True), "te")
            S["row_c_free"][cslot] = mk_bcm
            em.wait_mark("vector", mk_bcm)
            mk_rbc = em.op("vector", lambda e: e.tensor_copy(rbc[:, :], ps_bc[:, :]), "ve")
            S["free_bc"] = mk_rbc
            em.wait_mark("vector", S["free_atto"][aslot])
            mk_mul = em.op("vector", lambda e, uslot=uslot, aslot=aslot: e.tensor_mul(
                att_o[:, aslot * 512:(aslot + 1) * 512],
                att_u[:, uslot * 512:(uslot + 1) * 512], rbc[:, :]), "ve")
            em.wait_mark("sync", mk_mul)
            mk_ad = em.op("sync", lambda e, j=j, h=h, aslot=aslot: e.dma_start(
                out=agA_in[j // 2, h * 128:(h + 1) * 128,
                           (j % 2) * 512:(j % 2) * 512 + 512],
                in_=att_o[:, aslot * 512:(aslot + 1) * 512]), "st")
            S["free_atto"][aslot] = mk_ad

    def emit_attention(j):
        nkc = 4 * j + 4
        npair = nkc // 2
        for h in range(HPC):
            pend = None
            for pi in range(npair):
                bk = pi % 2
                tensorX = psA if bk == 0 else psB
                readers = S["readersA"] if bk == 0 else S["readersB"]
                for r in readers.values():
                    em.wait_mark("tensor", r)
                readers.clear()
                if pi == 0:
                    em.wait_mark("tensor", S["rope_mark"][h])
                mk_sp = None
                for half in (0, 1):
                    kc = 2 * pi + half
                    mk_sp = em.op("tensor", lambda e, h=h, kc=kc, tensorX=tensorX,
                                  half=half: e.matmul(
                        tensorX[:, half * 512:(half + 1) * 512],
                        kT[h][:, kc * 128:(kc + 1) * 128], qT[h][:, :],
                        start=True, stop=True), "te" if half == 1 else None)
                if pend is not None:
                    emit_pv_pair(*pend)
                if pi == 1:
                    flush_norm(1)
                    if S["pending_lnb"]:
                        emit_ln_back()
                if pi == 2 and S["pending_ln"]:
                    emit_ln_front(*S["pending_ln"].pop(0))
                slot = pi % 3
                em.wait_mark("scalar", mk_sp)
                em.wait_mark("scalar", S["free_expS"][slot])
                mk_e = em.op("scalar", lambda e, tensorX=tensorX, slot=slot: e.activation(
                    expS[:, slot * 1024:(slot + 1) * 1024], tensorX[:, 0:1024],
                    AF.Exp, scale=SCALE), "sc")
                readers["sc"] = mk_e
                mk_ready = mk_e
                for half in (0, 1):
                    kc = 2 * pi + half
                    if kc >= 4 * j:
                        em.wait_mark("gpsimd", mk_e)
                        m = kc - 4 * j
                        mk_ready = em.op("gpsimd", lambda e, slot=slot, half=half,
                                         m=m: e.affine_select(
                            expS[:, slot * 1024 + half * 512: slot * 1024 + (half + 1) * 512],
                            expS[:, slot * 1024 + half * 512: slot * 1024 + (half + 1) * 512],
                            pattern=[[1, 512]], compare_op=ALU.is_ge, fill=0.0,
                            base=-128 * m, channel_multiplier=-1), "gp")
                pend = (j, h, pi, slot, mk_ready, nkc)
            mk_pvlast = emit_pv_pair(*pend)
            uslot = h % 2
            aslot = (j * HPC + h) % 2
            em.wait_mark("vector", mk_pvlast)
            mk_cu = em.op("vector", lambda e, uslot=uslot: e.tensor_copy(
                att_u[:, uslot * 512:(uslot + 1) * 512], ps_pv[:, :]), "ve")
            S["free_pv"] = mk_cu
            cslot = h % 2
            em.wait_mark("vector", S["row_a_free"][cslot])
            mk_cs = em.op("vector", lambda e, cslot=cslot: e.tensor_copy(
                row_a[0:1, cslot * 512:(cslot + 1) * 512], ps_sum[0:1, :]), "ve")
            S["free_sum"] = mk_cs
            S["pending_norm"].append((j, h, mk_cs, uslot, aslot, cslot))
            if j + 1 < NB:
                S["pending_ln"].append((j + 1, h))
        while S["pending_lnb"]:
            emit_ln_back()
        while S["pending_ln"]:
            emit_ln_tile(*S["pending_ln"].pop(0))
        for hh in range(HPC):
            S["qT_free"][hh] = ("te", em.cnt["te"])

    S["row_c_free"] = [None, None]
    norm_last = None
    for tt in range(TPB):
        norm_last = emit_ln_tile(0, tt)
    ln_last = {0: norm_last}
    for j in range(NB):
        emit_xq(j, ln_last[j])
        emit_attention(j)
        if j % 2 == 1:
            flush_norm(0)
            rb = j // 2
            em.wait("gpsimd", "st")
            em.op("gpsimd", lambda e, rb=rb: e.collective_compute(
                "AllGather", ALU.bypass,
                replica_groups=[list(range(8))],
                ins=[agA_in[rb].opt()], outs=[agA_out[rb].opt()]), "cc")
        if j + 1 < NB:
            ln_last[j + 1] = S["free_stat"]
    flush_norm(0)
    cc_A = em.cnt["cc"]

    cc_A = em.cnt["cc"]

    if debug:
        em.wait("sync", "te")
        em.wait("sync", "ve")
        em.wait("sync", "sc")
        em.op("sync", lambda e: e.dma_start(out=dbg["xnT"], in_=xnT_dram), "st")
        em.op("sync", lambda e: e.dma_start(out=dbg["agA"], in_=agA_in), "st")
        em.op("sync", lambda e: e.dma_start(out=dbg["kT"], in_=kT[0][:, :]), "st")
        em.op("sync", lambda e: e.dma_start(out=dbg["v"], in_=v_s[:, :]), "st")
        for h in range(HPC):
            em.op("sync", lambda e, h=h: e.dma_start(
                out=dbg["qT"][:, h * 512:(h + 1) * 512], in_=qT[h][:, :]), "st")
        em.wait("gpsimd", "cc", cc_F)
        em.op("gpsimd", lambda e: e.dma_start(out=dbg["agF"], in_=agF_out), "gd")

    st_guard = em.cnt["st"]
    gd_guard = em.cnt["gd"]
    te_all = em.cnt["te"]
    ve_all = em.cnt["ve"]
    sc_all = em.cnt["sc"]
    ctx12.close()
    ctx3 = ExitStack()
    gath = [ctx3.enter_context(nc.sbuf_tensor(f"gath{s}", [128, 4 * 1024], BF16))
            for s in range(4)]
    xno = ctx3.enter_context(nc.sbuf_tensor("xno", [128, 16 * 1024], BF16))
    wo_nb = [ctx3.enter_context(nc.sbuf_tensor(f"wo_nb{i}", [128, 16 * 512], BF16))
             for i in range(2)]
    fo_nb = [ctx3.enter_context(nc.sbuf_tensor(f"fo_nb{i}", [128, 16 * 512], BF16))
             for i in range(2)]
    xres = [ctx3.enter_context(nc.sbuf_tensor(f"xres{i}", [128, D], F32))
            for i in range(2)]
    out_sb = [ctx3.enter_context(nc.sbuf_tensor(f"out_sb{i}", [128, 512], F32))
              for i in range(2)]

    for eng in ("sync", "gpsimd"):
        em.wait(eng, "cc", cc_A)
        em.wait(eng, "te", te_all)
        em.wait(eng, "ve", ve_all)
        em.wait(eng, "sc", sc_all)
        em.wait(eng, "st", st_guard)
        em.wait(eng, "gd", gd_guard)

    _idx_cache = {}

    def _p4(e):
        if "p4" not in _idx_cache:
            pid = e.snap(e.partition_id(), min_val=0, max_val=7)
            _idx_cache["p4"] = e.snap(pid % 4, min_val=0, max_val=3)
            _idx_cache["p8"] = e.snap(_idx_cache["p4"] * 8, min_val=0, max_val=24)
            g0 = ((pid % 8 - _idx_cache["p4"] + 8) % 8) % 5
            _idx_cache["g08"] = e.snap(_idx_cache["p4"] * 8 + g0,
                                       min_val=0, max_val=28)
        return _idx_cache

    gath_mk = []
    for s in range(4):
        gath_mk.append(em.op("sync", lambda e, s=s: e.dma_start(
            out=gath[s].ap().rearrange("p (hc t) -> p hc t", hc=4),
            in_=agA_out.rearrange("a b h t -> (a b) h t")[_p4(e)["g08"] + s
                        ].rearrange("(hc p) t -> p hc t", p=128)), "ld"))
    mk_xno = em.op("sync", lambda e: e.dma_start(
        out=xno.ap().rearrange("p (dc t) -> p dc t", dc=16),
        in_=xnT_dram.rearrange("d (blk t) -> blk d t", blk=4)[
            _p4(e)["p4"]
        ].rearrange("(dc p) t -> p dc t", p=128)), "ld")

    wo_mk = [None, None]
    fo_mk = [None, None]
    wo_used = [0, 0]
    xres_mk = [None, None]
    out_mk = [None, None]
    ev_used = [None, None]
    for nb in range(4):
        wsl = nb % 2
        if nb >= 2:
            em.wait("gpsimd", "te", wo_used[wsl])
            em.wait("sync", "te", wo_used[wsl])
        wo_mk[wsl] = em.op("gpsimd", lambda e, nb=nb, wsl=wsl: e.dma_start(
            out=wo_nb[wsl].ap().rearrange("p (hc od) -> p hc od", hc=16),
            in_=wout_in[:, nb * 512:(nb + 1) * 512].rearrange(
                "(hc p) od -> p hc od", p=128)), "gd")
        fo_mk[wsl] = em.op("sync", lambda e, nb=nb, wsl=wsl: e.dma_start(
            out=fo_nb[wsl].ap().rearrange("p (dc od) -> p dc od", dc=16),
            in_=agF_out[:, nb * 512:(nb + 1) * 512].rearrange(
                "(dc p) od -> p dc od", p=128)), "ld")
        for tt in range(8):
            psl = tt % 2
            pbank = ps_v if psl == 0 else ps_bc
            em.wait_mark("sync", ev_used[psl])
            xres_mk[psl] = em.op("sync", lambda e, tt=tt: e.dma_start(
                out=xres[tt % 2][:, :],
                in_=x_in.rearrange("(g r) d -> g r d", g=32)[
                    _p4(e)["p8"] + tt]), "ld")
            em.wait_mark("tensor", gath_mk[-1])
            em.wait_mark("tensor", mk_xno)
            em.wait_mark("tensor", wo_mk[wsl])
            em.wait_mark("tensor", fo_mk[wsl])
            em.wait_mark("tensor", ev_used[psl])
            first = True
            for hh in range(16):
                em.op("tensor", lambda e, hh=hh, tt=tt, wsl=wsl, pbank=pbank,
                      first=first: e.matmul(
                          pbank[:, :],
                          gath[hh // 4][:, (hh % 4) * 1024 + tt * 128:
                                        (hh % 4) * 1024 + (tt + 1) * 128],
                          wo_nb[wsl][:, hh * 512:(hh + 1) * 512],
                          start=first, stop=False), None)
                first = False
            for dc in range(16):
                em.op("tensor", lambda e, dc=dc, tt=tt, wsl=wsl, pbank=pbank: e.matmul(
                    pbank[:, :],
                    xno[:, dc * 1024 + tt * 128: dc * 1024 + (tt + 1) * 128],
                    fo_nb[wsl][:, dc * 512:(dc + 1) * 512],
                    start=False, stop=False), None)
            mk_mm = em.op("tensor", lambda e, nb=nb, pbank=pbank: e.matmul(
                pbank[:, :], ones_row_bf[:, :], bias2_bf[0:1, nb * 512:(nb + 1) * 512],
                start=False, stop=True), "te")
            wo_used[wsl] = em.cnt["te"]
            em.wait_mark("vector", mk_mm)
            em.wait_mark("vector", xres_mk[psl])
            em.wait_mark("vector", out_mk[psl])
            mk_ev = em.op("vector", lambda e, nb=nb, psl=psl, pbank=pbank:
                          e.tensor_add(out_sb[psl][:, :], pbank[:, :],
                                       xres[psl][:, nb * 512:(nb + 1) * 512]), "ve")
            ev_used[psl] = mk_ev
            em.wait_mark("sync", mk_ev)
            out_mk[psl] = em.op("sync", lambda e, nb=nb, tt=tt, psl=psl: e.dma_start(
                out=y_out[tt * 128:(tt + 1) * 128, nb * 512:(nb + 1) * 512],
                in_=out_sb[psl][:, :]), "st")

    em.wait("sync", "st")
    em.wait("sync", "ld")
    em.wait("sync", "gd")
    em.wait("sync", "te")
    em.wait("sync", "ve")
    em.wait("sync", "sc")
    em.wait("sync", "gp")
    em.wait("sync", "cc")

    with nc.Block() as block:
        em.emit(block)
    ctx3.close()
    ctx.close()
    return nc


_NC_CACHE = {}


def _get_nc(debug=False):
    key = bool(debug)
    if key not in _NC_CACHE:
        _NC_CACHE[key] = build_nc(debug=key)
    return _NC_CACHE[key]


def _perm_rows_for_head(g):
    base = 128 * g
    return [base + d for d in range(0, 128, 2)] + [base + d for d in range(1, 128, 2)]


def make_in_maps(L0_attn_x, L0_attn_w_qkv, L0_attn_b_qkv, L0_attn_w_out,
                 L0_attn_b_out, L0_ffn_factors, L0_ffn_bias):
    x = np.asarray(L0_attn_x, np.float32)
    w_qkv = np.asarray(L0_attn_w_qkv, np.float32)
    b_qkv = np.asarray(L0_attn_b_qkv, np.float32)
    w_out = np.asarray(L0_attn_w_out, np.float32)
    b_out = np.asarray(L0_attn_b_out, np.float32)
    factors = np.asarray(L0_ffn_factors, np.float32)
    ffn_bias = np.asarray(L0_ffn_bias, np.float32)

    woutT = np.ascontiguousarray(w_out.T)
    in_maps = []
    for c in range(8):
        p, b = c % 4, c // 4
        heads = [4 * p + i for i in range(HPC)]
        q_sel = sum((_perm_rows_for_head(g) for g in heads), [])
        k_sel = [2048 + r for r in q_sel]
        v_sel = sum(([4096 + 128 * g + d for d in range(128)] for g in heads), [])
        sel = q_sel + k_sel + v_sel
        wT = w_qkv[sel, :].T
        packed = np.ascontiguousarray(
            wT.reshape(16, 128, 1536).transpose(1, 0, 2).reshape(128, 16 * 1536))
        bsel = b_qkv[sel]
        bias_col = np.ascontiguousarray(bsel.reshape(12, 128).T)
        bqk = bsel[:1024].reshape(8, 128)
        bias_swap = np.ascontiguousarray(
            np.concatenate([bqk[:, 64:], bqk[:, :64]], axis=1).T)
        bias_vrow = np.ascontiguousarray(bsel[1024:1536].reshape(1, 512))
        in_maps.append({
            "x_in": np.ascontiguousarray(x[b]),
            "wqkv_in": packed,
            "bias_col_in": bias_col,
            "bias_swap_in": bias_swap,
            "bias_vrow_in": bias_vrow,
            "wout_in": woutT,
            "fac_in": np.ascontiguousarray(factors[:, 256 * c:256 * (c + 1), :]),
            "bout_in": np.ascontiguousarray(b_out.reshape(1, D)),
            "fbias_in": np.ascontiguousarray(ffn_bias.reshape(1, D)),
        })
    return in_maps


def kernel(L0_attn_x, L0_attn_w_qkv, L0_attn_b_qkv, L0_attn_w_out, L0_attn_b_out,
           L0_ffn_factors, L0_ffn_bias, _debug=False):
    in_maps = make_in_maps(L0_attn_x, L0_attn_w_qkv, L0_attn_b_qkv, L0_attn_w_out,
                           L0_attn_b_out, L0_ffn_factors, L0_ffn_bias)
    nc = _get_nc(debug=_debug)
    res = run_bass_kernel_spmd(nc, in_maps, core_ids=list(range(8)))
    out = np.empty((B, T, D), np.float32)
    for c in range(8):
        p, b = c % 4, c // 4
        out[b, 1024 * p:1024 * (p + 1), :] = res.results[c]["y_out"]
    if _debug:
        return out, res
    return out


# version 07 (final)
# speedup vs baseline: 1.0239x; 1.0239x over previous
import math
import numpy as np
import concourse.bass as bass
import concourse.mybir as mybir
from concourse.bass_utils import run_bass_kernel_spmd
from contextlib import ExitStack

F32 = mybir.dt.float32
BF16 = mybir.dt.bfloat16
I32 = mybir.dt.int32
AF = mybir.ActivationFunctionType
ALU = mybir.AluOpType

B, T, D = 2, 4096, 2048
H, DH = 16, 128
HPC = 4
NB = 8
TPB = 4
NKC = 32
SCALE = 1.0 / math.sqrt(DH)
ROWS = 1024
TWO_PI = 2.0 * math.pi
EPS = 1e-5
ENGINES = ("sync", "tensor", "vector", "scalar", "gpsimd")


class Emitter:
    def __init__(self, nc, ctx):
        self.nc = nc
        self.q = {e: [] for e in ENGINES}
        self.sem = {n: ctx.enter_context(nc.semaphore(n))
                    for n in ("ld", "st", "gd", "cc", "te", "ve", "sc", "gp")}
        self.cnt = {n: 0 for n in self.sem}

    def op(self, eng, fn, inc=None, amt=None):
        if inc is not None:
            if amt is None:
                amt = 16 if inc in ("ld", "st", "gd") else 1
            self.cnt[inc] += amt
            mark = (inc, self.cnt[inc])
        else:
            mark = None
        self.q[eng].append(("op", fn, inc, amt))
        return mark

    def wait(self, eng, sem, val=None):
        v = self.cnt[sem] if val is None else val
        if v <= 0:
            return
        self.q[eng].append(("wait", sem, v, None))

    def wait_mark(self, eng, mark):
        if mark is not None:
            self.wait(eng, mark[0], mark[1])

    def emit(self, block):
        for ename in ENGINES:
            items = self.q[ename]
            if not items:
                continue
            section = getattr(block, ename)

            def body(e, items=items):
                for it in items:
                    if it[0] == "op":
                        _, fn, inc, amt = it
                        instr = fn(e)
                        if inc is not None:
                            instr.then_inc(self.sem[inc], amt)
                    else:
                        _, sname, v, _ = it
                        e.wait_ge(self.sem[sname], v)
            section(body)


def build_nc(debug=False):
    nc = bass.Bass("TRN2", target_bir_lowering=False)

    x_in = nc.dram_tensor("x_in", [T, D], F32, kind="ExternalInput").ap()
    wqkv_in = nc.dram_tensor("wqkv_in", [128, 16 * 1536], F32, kind="ExternalInput").ap()
    bias_col_in = nc.dram_tensor("bias_col_in", [128, 12], F32, kind="ExternalInput").ap()
    bias_swap_in = nc.dram_tensor("bias_swap_in", [128, 8], F32, kind="ExternalInput").ap()
    bias_vrow_in = nc.dram_tensor("bias_vrow_in", [1, 512], F32, kind="ExternalInput").ap()
    wout_in = nc.dram_tensor("wout_in", [D, D], F32, kind="ExternalInput").ap()
    fac_in = nc.dram_tensor("fac_in", [8, 256, D], F32, kind="ExternalInput").ap()
    bout_in = nc.dram_tensor("bout_in", [1, D], F32, kind="ExternalInput").ap()
    fbias_in = nc.dram_tensor("fbias_in", [1, D], F32, kind="ExternalInput").ap()
    y_out = nc.dram_tensor("y_out", [ROWS, D], F32, kind="ExternalOutput").ap()

    xnT_dram = nc.dram_tensor("xnT_dram", [D, T], BF16).ap()
    tabs_dram = nc.dram_tensor("tabs_dram", [2, 128, T], F32).ap()
    agF_in = nc.dram_tensor("agF_in", [256, D], BF16).ap()
    agF_out = nc.dram_tensor("agF_out", [D, D], BF16).ap()
    agA_in = nc.dram_tensor("agA_in", [4, 512, 1024], BF16).ap()
    agA_out = nc.dram_tensor("agA_out", [4, 8, 512, 1024], BF16).ap()

    dbg = {}
    if debug:
        dbg["xnT"] = nc.dram_tensor("dbg_xnT", [D, T], BF16, kind="ExternalOutput").ap()
        dbg["agA"] = nc.dram_tensor("dbg_agA", [4, 512, 1024], BF16, kind="ExternalOutput").ap()
        dbg["agF"] = nc.dram_tensor("dbg_agF", [D, D], BF16, kind="ExternalOutput").ap()
        dbg["kT"] = nc.dram_tensor("dbg_kT", [128, T], BF16, kind="ExternalOutput").ap()
        dbg["v"] = nc.dram_tensor("dbg_v", [128, NKC * 512], BF16, kind="ExternalOutput").ap()
        dbg["qT"] = nc.dram_tensor("dbg_qT", [128, HPC * 512], BF16, kind="ExternalOutput").ap()

    ctx = ExitStack()
    em = Emitter(nc, ctx)

    psA = ctx.enter_context(nc.psum_tensor("psA", [128, 1024], F32))
    psB = ctx.enter_context(nc.psum_tensor("psB", [128, 1024], F32))
    ps_pv = ctx.enter_context(nc.psum_tensor("ps_pv", [128, 512], F32))
    ps_sum = ctx.enter_context(nc.psum_tensor("ps_sum", [1, 512], F32))
    ps_v = ctx.enter_context(nc.psum_tensor("ps_v", [128, 512], F32))
    ps_bc = ctx.enter_context(nc.psum_tensor("ps_bc", [128, 512], F32))

    ones_col_bf = ctx.enter_context(nc.sbuf_tensor("ones_col_bf", [128, 1], BF16))
    ones_row_bf = ctx.enter_context(nc.sbuf_tensor("ones_row_bf", [1, 128], BF16))
    ones_row_f = ctx.enter_context(nc.sbuf_tensor("ones_row_f", [1, 128], F32))
    ident_bf = ctx.enter_context(nc.sbuf_tensor("ident_bf", [128, 128], BF16))
    ones128_bf = ctx.enter_context(nc.sbuf_tensor("ones128_bf", [128, 128], BF16))
    bias_col = ctx.enter_context(nc.sbuf_tensor("bias_col", [128, 12], F32))
    bias_swap = ctx.enter_context(nc.sbuf_tensor("bias_swap", [128, 8], F32))
    bias_v_bf = ctx.enter_context(nc.sbuf_tensor("bias_v_bf", [1, 512], BF16))
    bias2_bf = ctx.enter_context(nc.sbuf_tensor("bias2_bf", [1, D], BF16))
    row_a = ctx.enter_context(nc.sbuf_tensor("row_a", [1, 1024], F32))
    row_c = ctx.enter_context(nc.sbuf_tensor("row_c", [1, 1024], F32))
    frq_a = ctx.enter_context(nc.sbuf_tensor("frq_a", [1, 128], F32))
    frq_b = ctx.enter_context(nc.sbuf_tensor("frq_b", [1, 128], F32))
    iota_blk = ctx.enter_context(nc.sbuf_tensor("iota_blk", [1, 512], F32))
    stat = ctx.enter_context(nc.sbuf_tensor("stat", [128, 8], F32))

    ctx12 = ExitStack()
    wq_s = ctx12.enter_context(nc.sbuf_tensor("wq_s", [128, 16 * 1536], BF16))
    kT = [ctx12.enter_context(nc.sbuf_tensor(f"kT{h}", [128, T], BF16)) for h in range(HPC)]
    v_s = ctx12.enter_context(nc.sbuf_tensor("v_s", [128, NKC * 512], BF16))
    xstage = ctx12.enter_context(nc.sbuf_tensor("xstage", [128, 2 * D], F32))
    qT = [ctx12.enter_context(nc.sbuf_tensor(f"qT{h}", [128, 512], BF16)) for h in range(HPC)]
    rs_a = ctx12.enter_context(nc.sbuf_tensor("rs_a", [128, 512], F32))
    rs_b = ctx12.enter_context(nc.sbuf_tensor("rs_b", [128, 512], F32))
    expS = ctx12.enter_context(nc.sbuf_tensor("expS", [128, 3 * 1024], BF16))
    att_o = ctx12.enter_context(nc.sbuf_tensor("att_o", [128, 2 * 512], BF16))
    att_u = ctx12.enter_context(nc.sbuf_tensor("att_u", [128, 2 * 512], F32))
    rbc = ctx12.enter_context(nc.sbuf_tensor("rbc", [128, 512], F32))
    ln_sq = ctx12.enter_context(nc.sbuf_tensor("ln_sq", [128, D], BF16))
    cos_blk = ctx12.enter_context(nc.sbuf_tensor("cos_blk", [128, 512], F32))
    sin_blk = ctx12.enter_context(nc.sbuf_tensor("sin_blk", [128, 512], F32))
    ctxF = ExitStack()
    facc = ctxF.enter_context(nc.sbuf_tensor("facc", [128, 2 * D], F32))
    fshard = ctxF.enter_context(nc.sbuf_tensor("fshard", [128, 2 * D], BF16))
    ctxF.close()
    xn_tm = ctx12.enter_context(nc.sbuf_tensor("xn_tm", [128, TPB * D], BF16))
    xnT_j = ctx12.enter_context(nc.sbuf_tensor("xnT_j", [128, 16 * 512], BF16))

    em.op("sync", lambda e: e.dma_start(out=bias_col[:, :], in_=bias_col_in), "ld")
    em.op("sync", lambda e: e.dma_start(out=bias_swap[:, :], in_=bias_swap_in), "ld")
    mk_bv = em.op("sync", lambda e: e.dma_start(out=row_a[0:1, 0:512], in_=bias_vrow_in), "ld")
    mk_bo = em.op("sync", lambda e: e.dma_start(out=xstage[0:1, 0:D], in_=bout_in), "ld")
    em.op("gpsimd", lambda e: e.memset(ones_col_bf[:, :], 1.0), "gp")
    em.op("gpsimd", lambda e: e.memset(ones_row_bf[:, :], 1.0), "gp")
    em.op("gpsimd", lambda e: e.memset(ones_row_f[:, :], 1.0), "gp")
    em.op("gpsimd", lambda e: e.memset(ones128_bf[:, :], 1.0), "gp")
    em.op("gpsimd", lambda e: e.memset(stat[:, 7:8], EPS), "gp")
    mk_id = em.op("gpsimd", lambda e: e.affine_select(
        ident_bf[:, :], ones128_bf[:, :], pattern=[[1, 128]], compare_op=ALU.is_equal,
        fill=0.0, base=0, channel_multiplier=-1), "gp")

    mk_wq = None
    for s in range(8):
        mk_wq = em.op("gpsimd", lambda e, s=s: e.dma_start(
            out=wq_s[:, s * 3072:(s + 1) * 3072],
            in_=wqkv_in[:, s * 3072:(s + 1) * 3072]), "gd")

    em.wait_mark("gpsimd", mk_bo)
    mk_b2 = em.op("gpsimd", lambda e: e.dma_start(out=xstage[0:1, 0:D], in_=fbias_in,
                                                  accum_op=ALU.add), "gd")
    em.wait_mark("vector", mk_b2)
    mk_b2c = em.op("vector", lambda e: e.tensor_copy(bias2_bf[:, :], xstage[0:1, 0:D]), "ve")
    em.wait_mark("vector", mk_bv)
    mk_bvc = em.op("vector", lambda e: e.tensor_copy(bias_v_bf[:, :], row_a[0:1, 0:512]), "ve")

    fmk = []
    for r in range(8):
        if r > 0:
            em.wait("gpsimd", "gd")
        fmk.append(em.op("gpsimd", lambda e, r=r: e.dma_start(
            out=facc.ap().rearrange("p (c d) -> p c d", c=2),
            in_=fac_in[r].rearrange("(c p) d -> p c d", p=128),
            accum_op=(ALU.bypass if r == 0 else ALU.add)), "gd"))
    em.wait_mark("vector", fmk[-1])
    mk_fsc = em.op("vector", lambda e: e.tensor_scalar(
        fshard[:, :], facc[:, :], 1.0 / math.sqrt(8.0), None, ALU.mult), "ve")
    em.wait_mark("sync", mk_fsc)
    mk_fst = em.op("sync", lambda e: e.dma_start(
        out=agF_in.rearrange("(c p) d -> p c d", p=128),
        in_=fshard.ap().rearrange("p (c d) -> p c d", c=2)), "st")
    em.wait_mark("gpsimd", mk_fst)
    em.op("gpsimd", lambda e: e.collective_compute(
        "AllGather", ALU.bypass, replica_groups=[list(range(8))],
        ins=[agF_in.opt()], outs=[agF_out.opt()]), "cc")
    cc_F = em.cnt["cc"]

    em.op("gpsimd", lambda e: e.iota(frq_a[:, :], pattern=[[1, 128]], base=0,
                                     channel_multiplier=0,
                                     allow_small_or_imprecise_dtypes=True), "gp")
    em.wait("vector", "gp")
    em.op("vector", lambda e: e.tensor_scalar(frq_b[:, :], frq_a[:, :], 64.0, -64.0,
                                              ALU.is_ge, ALU.mult), "ve")
    em.op("vector", lambda e: e.drain(), None)
    mk_fq = em.op("vector", lambda e: e.tensor_add(frq_a[:, :], frq_a[:, :], frq_b[:, :]), "ve")
    em.wait_mark("scalar", mk_fq)
    mk_fe = em.op("scalar", lambda e: e.activation(frq_b[:, :], frq_a[:, :], AF.Exp,
                                                   scale=-math.log(10000.0) / 64.0), "sc")
    mk_tabs_done = None
    prev_r = None
    prev_th = None
    for ch in range(8):
        em.wait_mark("gpsimd", prev_th)
        mk_it = em.op("gpsimd", lambda e, ch=ch: e.iota(
            iota_blk[:, :], pattern=[[1, 512]], base=512 * ch,
            channel_multiplier=0, allow_small_or_imprecise_dtypes=True), "gp")
        em.wait_mark("tensor", mk_fe)
        em.wait_mark("tensor", mk_it)
        em.wait_mark("tensor", prev_r)
        mk_th = em.op("tensor", lambda e: e.matmul(
            ps_v[:, :], frq_b[:, :], iota_blk[:, :], start=True, stop=True), "te")
        prev_th = mk_th
        em.wait_mark("vector", mk_th)
        em.op("vector", lambda e: e.tensor_scalar(rs_a[:, :], ps_v[:, :], 1.0 / TWO_PI,
                                                  None, ALU.mult), "ve")
        em.op("vector", lambda e: e.tensor_copy(rs_b.ap().bitcast(I32), rs_a[:, :]), "ve")
        em.op("vector", lambda e: e.tensor_copy(rs_a[:, :], rs_b.ap().bitcast(I32)), "ve")
        mk_r = em.op("vector", lambda e: e.scalar_tensor_tensor(
            rs_b[:, :], rs_a[:, :], -TWO_PI, ps_v[:, :], ALU.mult, ALU.add), "ve")
        prev_r = mk_r
        em.op("vector", lambda e: e.tensor_scalar(rs_a[:, :], rs_b[:, :], math.pi / 2,
                                                  None, ALU.add), "ve")
        em.op("vector", lambda e: e.tensor_scalar(rbc[:, :], rs_a[:, :], math.pi,
                                                  -TWO_PI, ALU.is_gt, ALU.mult), "ve")
        mk_r2 = em.op("vector", lambda e: e.tensor_add(rs_a[:, :], rs_a[:, :], rbc[:, :]), "ve")
        em.wait_mark("scalar", mk_r2)
        mk_sin = em.op("scalar", lambda e: e.activation(rbc[:, :], rs_b[:, :], AF.Sin), "sc")
        mk_cos = em.op("scalar", lambda e: e.activation(rs_b[:, :], rs_a[:, :], AF.Sin), "sc")
        em.wait_mark("sync", mk_sin)
        em.op("sync", lambda e, ch=ch: e.dma_start(
            out=tabs_dram[1, :, ch * 512:(ch + 1) * 512], in_=rbc[:, :]), "st")
        em.wait_mark("sync", mk_cos)
        mk_tabs_done = em.op("sync", lambda e, ch=ch: e.dma_start(
            out=tabs_dram[0, :, ch * 512:(ch + 1) * 512], in_=rs_b[:, :]), "st")
        em.wait_mark("vector", mk_tabs_done)

    def psA_bfv():
        return psA.ap().bitcast(BF16)

    S = {
        "free_pv": None, "free_sum": None, "free_v": prev_r, "free_bc": None,
        "free_expS": [None, None, None], "free_atto": [None, None],
        "free_xstage": [mk_b2c, None], "free_stat": None, "prev_rstd": None,
        "free_cos": None, "free_rowa": None, "xpose_done": None,
        "mk_xnT_store": None, "readersA": {"gp": mk_id}, "readersB": {"gp": mk_id},
        "qT_free": [None] * HPC, "rope_mark": [None] * HPC,
        "cos_ready": None, "sin_ready": None, "pending_norm": [],
        "pending_ln": [], "pending_lnb": [], "row_a_free": [None, None],
        "last_scores_te": None,
    }
    vev_mark = {}

    def emit_ln_front(j, tt):
        sl = tt % 2
        em.wait_mark("sync", S["free_xstage"][sl])
        mk_x = em.op("sync", lambda e, j=j, tt=tt, sl=sl: e.dma_start(
            out=xstage[:, sl * D:(sl + 1) * D],
            in_=x_in[j * 512 + tt * 128: j * 512 + (tt + 1) * 128, :]), "ld")
        em.wait_mark("vector", mk_x)
        if j == 0 and tt == 0:
            em.wait_mark("vector", mk_fst)
        em.wait_mark("vector", S["xpose_done"])
        em.wait_mark("vector", S["prev_rstd"])
        mk_s1 = em.op("vector", lambda e, tt=tt, sl=sl: e.tensor_scalar(
            xn_tm[:, tt * D:(tt + 1) * D], xstage[:, sl * D:(sl + 1) * D],
            1.0, 0.0, ALU.mult, ALU.add, accum_out=stat[:, 0:1]), "ve")
        mk_s2 = em.op("vector", lambda e, sl=sl: e.scalar_tensor_tensor(
            ln_sq[:, :], xstage[:, sl * D:(sl + 1) * D], 0.0,
            xstage[:, sl * D:(sl + 1) * D], ALU.add, ALU.mult,
            accum_out=stat[:, 1:2]), "ve")
        S["pending_lnb"].append((j, tt, sl, mk_s2))

    def emit_ln_back():
        (j, tt, sl, mk_s2) = S["pending_lnb"].pop(0)
        em.wait_mark("gpsimd", mk_s2)
        em.wait_mark("gpsimd", S["free_stat"])
        if j == 0 and tt == 0:
            em.wait_mark("gpsimd", mk_id)
        em.op("gpsimd", lambda e: e.tensor_scalar(stat[:, 3:4], stat[:, 0:1],
                                                  -1.0 / D, 0.0, ALU.mult, ALU.add), "gp")
        em.op("gpsimd", lambda e: e.tensor_scalar(stat[:, 4:5], stat[:, 1:2],
                                                  1.0 / D, EPS, ALU.mult, ALU.add), "gp")
        em.op("gpsimd", lambda e: e.tensor_tensor(stat[:, 6:7], stat[:, 3:4],
                                                  stat[:, 3:4], ALU.mult), "gp")
        mk_var = em.op("gpsimd", lambda e: e.tensor_tensor(stat[:, 5:6], stat[:, 4:5],
                                                           stat[:, 6:7], ALU.subtract), "gp")
        em.wait_mark("scalar", mk_var)
        em.op("scalar", lambda e: e.activation(stat[:, 5:6], stat[:, 5:6], AF.Ln), "sc")
        em.op("scalar", lambda e: e.activation(stat[:, 5:6], stat[:, 5:6],
                                               AF.Exp, scale=-0.5), "sc")
        mk_rstd = em.op("scalar", lambda e: e.copy(stat[:, 2:3], stat[:, 5:6]), "sc")
        S["prev_rstd"] = mk_rstd
        em.wait_mark("vector", mk_rstd)
        mk_norm = em.op("vector", lambda e, tt=tt, sl=sl: e.tensor_scalar(
            xn_tm[:, tt * D:(tt + 1) * D], xstage[:, sl * D:(sl + 1) * D],
            stat[:, 3:4], stat[:, 5:6], ALU.add, ALU.mult), "ve")
        S["free_xstage"][sl] = mk_norm
        S["free_stat"] = mk_norm
        return mk_norm

    def emit_ln_tile(j, tt):
        emit_ln_front(j, tt)
        return emit_ln_back()

    def emit_xq(j, norm_last):
        em.wait_mark("sync", mk_tabs_done)
        em.wait_mark("sync", S["free_cos"])
        S["cos_ready"] = em.op("sync", lambda e, j=j: e.dma_start(
            out=cos_blk[:, :], in_=tabs_dram[0, :, j * 512:(j + 1) * 512]), "ld")
        S["sin_ready"] = em.op("sync", lambda e, j=j: e.dma_start(
            out=sin_blk[:, :], in_=tabs_dram[1, :, j * 512:(j + 1) * 512]), "ld")
        em.wait_mark("tensor", norm_last)
        if S["mk_xnT_store"] is not None:
            em.wait_mark("tensor", S["mk_xnT_store"])
        for r in S["readersA"].values():
            em.wait_mark("tensor", r)
        S["readersA"] = {}
        ev_marks = []
        mk_tp_last = None
        for dc in range(16):
            half = dc % 2
            if dc >= 2:
                em.wait_mark("tensor", ev_marks[dc - 2])
            for tt in range(TPB):
                mk_tp = em.op("tensor", lambda e, dc=dc, tt=tt, half=half: e.transpose(
                    psA_bfv()[:, half * 1024 + tt * 128: half * 1024 + (tt + 1) * 128],
                    xn_tm[:, tt * D + dc * 128: tt * D + (dc + 1) * 128],
                    ident_bf[:, :]), "te" if tt == 3 else None)
            mk_tp_last = mk_tp
            em.wait_mark("vector", mk_tp)
            ev = em.op("vector", lambda e, dc=dc, half=half: e.tensor_copy(
                xnT_j[:, dc * 512:(dc + 1) * 512],
                psA_bfv()[:, half * 1024: half * 1024 + 512]), "ve")
            ev_marks.append(ev)
        S["xpose_done"] = mk_tp_last
        S["readersA"]["ve"] = ev_marks[-1]
        em.wait_mark("sync", ev_marks[-1])
        S["mk_xnT_store"] = em.op("sync", lambda e, j=j: e.dma_start(
            out=xnT_dram.rearrange("(dc p) t -> p dc t", p=128)[:, :, j * 512:(j + 1) * 512],
            in_=xnT_j.ap().rearrange("p (dc t) -> p dc t", dc=16)), "st")

        em.wait_mark("tensor", mk_wq)
        slot_free = [None, None]
        for cc in range(8):
            bk = cc % 2
            tensorX = psA if bk == 0 else psB
            readers = S["readersA"] if bk == 0 else S["readersB"]
            for r in readers.values():
                em.wait_mark("tensor", r)
            readers.clear()
            em.wait_mark("tensor", slot_free[bk])
            mk_mm = None
            for dc in range(16):
                mk_mm = em.op("tensor", lambda e, cc=cc, dc=dc, tensorX=tensorX: e.matmul(
                    tensorX[:, 0:512],
                    wq_s[:, dc * 1536 + cc * 128: dc * 1536 + (cc + 1) * 128],
                    xnT_j[:, dc * 512:(dc + 1) * 512],
                    start=(dc == 0), stop=(dc == 15)), "te" if dc == 15 else None)
            is_q = cc < 4
            h = cc % 4
            dest = qT[h] if is_q else kT[h]
            dst_off = 0 if is_q else j * 512
            em.wait_mark("vector", mk_mm)
            em.wait_mark("vector", S["cos_ready"])
            em.wait_mark("vector", S["sin_ready"])
            if cc == 0 and j == 0:
                em.wait_mark("vector", mk_tabs_done)
            if is_q:
                em.wait_mark("vector", S["qT_free"][h])
            pb = tensorX
            em.op("vector", lambda e, pb=pb, cc=cc: e.scalar_tensor_tensor(
                rs_a[0:64, :], pb[0:64, 0:512], bias_col[0:64, cc:cc + 1],
                cos_blk[0:64, :], ALU.add, ALU.mult), "ve")
            em.op("vector", lambda e, pb=pb, cc=cc: e.scalar_tensor_tensor(
                rs_b[0:64, :], pb[64:128, 0:512], bias_swap[0:64, cc:cc + 1],
                sin_blk[0:64, :], ALU.add, ALU.mult), "ve")
            em.op("vector", lambda e, dest=dest, dst_off=dst_off: e.tensor_sub(
                dest[0:64, dst_off:dst_off + 512], rs_a[0:64, :], rs_b[0:64, :]), "ve")
            em.op("vector", lambda e, pb=pb, cc=cc: e.scalar_tensor_tensor(
                rs_a[64:128, :], pb[0:64, 0:512], bias_swap[64:128, cc:cc + 1],
                sin_blk[64:128, :], ALU.add, ALU.mult), "ve")
            em.op("vector", lambda e, pb=pb, cc=cc: e.scalar_tensor_tensor(
                rs_b[64:128, :], pb[64:128, 0:512], bias_col[64:128, cc:cc + 1],
                cos_blk[64:128, :], ALU.add, ALU.mult), "ve")
            mk_rope = em.op("vector", lambda e, dest=dest, dst_off=dst_off: e.tensor_add(
                dest[64:128, dst_off:dst_off + 512], rs_a[64:128, :], rs_b[64:128, :]), "ve")
            slot_free[bk] = mk_rope
            readers["ve"] = mk_rope
            S["rope_mark"][h] = mk_rope
        S["free_cos"] = S["rope_mark"][3]
        for tt in range(TPB):
            em.wait_mark("tensor", S["free_v"])
            for dc in range(16):
                em.op("tensor", lambda e, tt=tt, dc=dc: e.matmul(
                    ps_v[:, :],
                    xnT_j[:, dc * 512 + tt * 128: dc * 512 + (tt + 1) * 128],
                    wq_s[:, dc * 1536 + 1024: dc * 1536 + 1536],
                    start=(dc == 0), stop=False), None)
            mk_mm = em.op("tensor", lambda e: e.matmul(
                ps_v[:, :], ones_row_bf[:, :], bias_v_bf[:, :],
                start=False, stop=True), "te")
            em.wait_mark("vector", mk_mm)
            if j == 0 and tt == 0:
                em.wait_mark("vector", mk_bvc)
            g = 4 * j + tt
            mk_vev = em.op("vector", lambda e, g=g: e.tensor_copy(
                v_s[:, g * 512:(g + 1) * 512], ps_v[:, :]), "ve")
            S["free_v"] = mk_vev
            vev_mark[g] = mk_vev

    def emit_pv_pair(j, h, pi, slot, mk_ready, nkc):
        em.wait_mark("tensor", mk_ready)
        if pi == 0:
            em.wait_mark("tensor", S["free_pv"])
            em.wait_mark("tensor", S["free_sum"])
            em.wait_mark("tensor", vev_mark[nkc - 1])
        mk = None
        for half in (0, 1):
            kc = 2 * pi + half
            em.op("tensor", lambda e, h=h, kc=kc, slot=slot, half=half, nkc=nkc: e.matmul(
                ps_pv[:, :], v_s[:, kc * 512 + h * 128: kc * 512 + (h + 1) * 128],
                expS[:, slot * 1024 + half * 512: slot * 1024 + (half + 1) * 512],
                start=(kc == 0), stop=(kc == nkc - 1)), None)
            mk = em.op("tensor", lambda e, kc=kc, slot=slot, half=half, nkc=nkc: e.matmul(
                ps_sum[0:1, :], ones_col_bf[:, 0:1],
                expS[:, slot * 1024 + half * 512: slot * 1024 + (half + 1) * 512],
                start=(kc == 0), stop=(kc == nkc - 1)), "te")
        S["free_expS"][slot] = mk
        return mk

    def flush_norm(limit):
        while len(S["pending_norm"]) > limit:
            (j, h, mk_cs, uslot, aslot, cslot) = S["pending_norm"].pop(0)
            em.wait_mark("scalar", mk_cs)
            em.op("scalar", lambda e, cslot=cslot: e.activation(
                row_a[0:1, cslot * 512:(cslot + 1) * 512],
                row_a[0:1, cslot * 512:(cslot + 1) * 512], AF.Ln), "sc")
            em.wait_mark("scalar", S["row_c_free"][cslot])
            mk_rec = em.op("scalar", lambda e, cslot=cslot: e.activation(
                row_c[0:1, cslot * 512:(cslot + 1) * 512],
                row_a[0:1, cslot * 512:(cslot + 1) * 512],
                AF.Exp, scale=-1.0), "sc")
            S["row_a_free"][cslot] = mk_rec
            em.wait_mark("tensor", mk_rec)
            em.wait_mark("tensor", S["free_bc"])
            mk_bcm = em.op("tensor", lambda e, cslot=cslot: e.matmul(
                ps_bc[:, :], ones_row_f[:, :],
                row_c[0:1, cslot * 512:(cslot + 1) * 512], start=True,
                stop=True), "te")
            S["row_c_free"][cslot] = mk_bcm
            em.wait_mark("vector", mk_bcm)
            mk_rbc = em.op("vector", lambda e: e.tensor_copy(rbc[:, :], ps_bc[:, :]), "ve")
            S["free_bc"] = mk_rbc
            em.wait_mark("vector", S["free_atto"][aslot])
            mk_mul = em.op("vector", lambda e, uslot=uslot, aslot=aslot: e.tensor_mul(
                att_o[:, aslot * 512:(aslot + 1) * 512],
                att_u[:, uslot * 512:(uslot + 1) * 512], rbc[:, :]), "ve")
            em.wait_mark("sync", mk_mul)
            mk_ad = em.op("sync", lambda e, j=j, h=h, aslot=aslot: e.dma_start(
                out=agA_in[j // 2, h * 128:(h + 1) * 128,
                           (j % 2) * 512:(j % 2) * 512 + 512],
                in_=att_o[:, aslot * 512:(aslot + 1) * 512]), "st")
            S["free_atto"][aslot] = mk_ad

    def emit_attention(j):
        nkc = 4 * j + 4
        npair = nkc // 2
        for h in range(HPC):
            pend = None
            for pi in range(npair):
                bk = pi % 2
                tensorX = psA if bk == 0 else psB
                readers = S["readersA"] if bk == 0 else S["readersB"]
                for r in readers.values():
                    em.wait_mark("tensor", r)
                readers.clear()
                if pi == 0:
                    em.wait_mark("tensor", S["rope_mark"][h])
                mk_sp = None
                for half in (0, 1):
                    kc = 2 * pi + half
                    mk_sp = em.op("tensor", lambda e, h=h, kc=kc, tensorX=tensorX,
                                  half=half: e.matmul(
                        tensorX[:, half * 512:(half + 1) * 512],
                        kT[h][:, kc * 128:(kc + 1) * 128], qT[h][:, :],
                        start=True, stop=True), "te" if half == 1 else None)
                if pend is not None:
                    emit_pv_pair(*pend)
                if pi == 1:
                    flush_norm(1)
                    if S["pending_lnb"]:
                        emit_ln_back()
                if pi == 2 and S["pending_ln"]:
                    emit_ln_front(*S["pending_ln"].pop(0))
                slot = pi % 3
                em.wait_mark("scalar", mk_sp)
                em.wait_mark("scalar", S["free_expS"][slot])
                mk_e = em.op("scalar", lambda e, tensorX=tensorX, slot=slot: e.activation(
                    expS[:, slot * 1024:(slot + 1) * 1024], tensorX[:, 0:1024],
                    AF.Exp, scale=SCALE), "sc")
                readers["sc"] = mk_e
                mk_ready = mk_e
                for half in (0, 1):
                    kc = 2 * pi + half
                    if kc >= 4 * j:
                        em.wait_mark("gpsimd", mk_e)
                        m = kc - 4 * j
                        mk_ready = em.op("gpsimd", lambda e, slot=slot, half=half,
                                         m=m: e.affine_select(
                            expS[:, slot * 1024 + half * 512: slot * 1024 + (half + 1) * 512],
                            expS[:, slot * 1024 + half * 512: slot * 1024 + (half + 1) * 512],
                            pattern=[[1, 512]], compare_op=ALU.is_ge, fill=0.0,
                            base=-128 * m, channel_multiplier=-1), "gp")
                pend = (j, h, pi, slot, mk_ready, nkc)
            mk_pvlast = emit_pv_pair(*pend)
            uslot = h % 2
            aslot = (j * HPC + h) % 2
            em.wait_mark("vector", mk_pvlast)
            mk_cu = em.op("vector", lambda e, uslot=uslot: e.tensor_copy(
                att_u[:, uslot * 512:(uslot + 1) * 512], ps_pv[:, :]), "ve")
            S["free_pv"] = mk_cu
            cslot = h % 2
            em.wait_mark("vector", S["row_a_free"][cslot])
            mk_cs = em.op("vector", lambda e, cslot=cslot: e.tensor_copy(
                row_a[0:1, cslot * 512:(cslot + 1) * 512], ps_sum[0:1, :]), "ve")
            S["free_sum"] = mk_cs
            S["pending_norm"].append((j, h, mk_cs, uslot, aslot, cslot))
            if j + 1 < NB:
                S["pending_ln"].append((j + 1, h))
        while S["pending_lnb"]:
            emit_ln_back()
        while S["pending_ln"]:
            emit_ln_tile(*S["pending_ln"].pop(0))
        for hh in range(HPC):
            S["qT_free"][hh] = ("te", em.cnt["te"])

    S["row_c_free"] = [None, None]
    norm_last = None
    for tt in range(TPB):
        norm_last = emit_ln_tile(0, tt)
    ln_last = {0: norm_last}
    for j in range(NB):
        emit_xq(j, ln_last[j])
        emit_attention(j)
        if j % 2 == 1:
            flush_norm(0)
            rb = j // 2
            em.wait("gpsimd", "st")
            em.op("gpsimd", lambda e, rb=rb: e.collective_compute(
                "AllGather", ALU.bypass,
                replica_groups=[list(range(8))],
                ins=[agA_in[rb].opt()], outs=[agA_out[rb].opt()]), "cc")
        if j + 1 < NB:
            ln_last[j + 1] = S["free_stat"]
    flush_norm(0)
    cc_A = em.cnt["cc"]

    cc_A = em.cnt["cc"]

    if debug:
        em.wait("sync", "te")
        em.wait("sync", "ve")
        em.wait("sync", "sc")
        em.op("sync", lambda e: e.dma_start(out=dbg["xnT"], in_=xnT_dram), "st")
        em.op("sync", lambda e: e.dma_start(out=dbg["agA"], in_=agA_in), "st")
        em.op("sync", lambda e: e.dma_start(out=dbg["kT"], in_=kT[0][:, :]), "st")
        em.op("sync", lambda e: e.dma_start(out=dbg["v"], in_=v_s[:, :]), "st")
        for h in range(HPC):
            em.op("sync", lambda e, h=h: e.dma_start(
                out=dbg["qT"][:, h * 512:(h + 1) * 512], in_=qT[h][:, :]), "st")
        em.wait("gpsimd", "cc", cc_F)
        em.op("gpsimd", lambda e: e.dma_start(out=dbg["agF"], in_=agF_out), "gd")

    st_guard = em.cnt["st"]
    gd_guard = em.cnt["gd"]
    te_all = em.cnt["te"]
    ve_all = em.cnt["ve"]
    sc_all = em.cnt["sc"]
    ctx12.close()
    ctx3 = ExitStack()
    gath = [ctx3.enter_context(nc.sbuf_tensor(f"gath{s}", [128, 4 * 1024], BF16))
            for s in range(4)]
    xno = ctx3.enter_context(nc.sbuf_tensor("xno", [128, 16 * 1024], BF16))
    wo_nb = [ctx3.enter_context(nc.sbuf_tensor(f"wo_nb{i}", [128, 16 * 512], BF16))
             for i in range(2)]
    fo_nb = [ctx3.enter_context(nc.sbuf_tensor(f"fo_nb{i}", [128, 16 * 512], BF16))
             for i in range(2)]
    xres = [ctx3.enter_context(nc.sbuf_tensor(f"xres{i}", [128, D], F32))
            for i in range(2)]
    out_sb = [ctx3.enter_context(nc.sbuf_tensor(f"out_sb{i}", [128, 512], F32))
              for i in range(2)]

    for eng in ("sync", "gpsimd"):
        em.wait(eng, "te", te_all)
        em.wait(eng, "ve", ve_all)
        em.wait(eng, "sc", sc_all)
        em.wait(eng, "st", st_guard)
        em.wait(eng, "gd", gd_guard)

    _idx_cache = {}

    def _p4(e):
        if "p4" not in _idx_cache:
            pid = e.snap(e.partition_id(), min_val=0, max_val=7)
            _idx_cache["p4"] = e.snap(pid % 4, min_val=0, max_val=3)
            _idx_cache["p8"] = e.snap(_idx_cache["p4"] * 8, min_val=0, max_val=24)
            g0 = ((pid % 8 - _idx_cache["p4"] + 8) % 8) % 5
            _idx_cache["g08"] = e.snap(_idx_cache["p4"] * 8 + g0,
                                       min_val=0, max_val=28)
        return _idx_cache

    wo_mk = [None, None]
    fo_mk = [None, None]
    for nb in range(2):
        wo_mk[nb] = em.op("gpsimd", lambda e, nb=nb: e.dma_start(
            out=wo_nb[nb].ap().rearrange("p (hc od) -> p hc od", hc=16),
            in_=wout_in[:, nb * 512:(nb + 1) * 512].rearrange(
                "(hc p) od -> p hc od", p=128)), "gd")
    em.wait("sync", "cc", cc_F)
    for nb in range(2):
        fo_mk[nb] = em.op("sync", lambda e, nb=nb: e.dma_start(
            out=fo_nb[nb].ap().rearrange("p (dc od) -> p dc od", dc=16),
            in_=agF_out[:, nb * 512:(nb + 1) * 512].rearrange(
                "(dc p) od -> p dc od", p=128)), "ld")
    mk_xno = em.op("sync", lambda e: e.dma_start(
        out=xno.ap().rearrange("p (dc t) -> p dc t", dc=16),
        in_=xnT_dram.rearrange("d (blk t) -> blk d t", blk=4)[
            _p4(e)["p4"]
        ].rearrange("(dc p) t -> p dc t", p=128)), "ld")
    em.wait("sync", "cc", cc_A)
    gath_mk = []
    for s in range(4):
        gath_mk.append(em.op("sync", lambda e, s=s: e.dma_start(
            out=gath[s].ap().rearrange("p (hc t) -> p hc t", hc=4),
            in_=agA_out.rearrange("a b h t -> (a b) h t")[_p4(e)["g08"] + s
                        ].rearrange("(hc p) t -> p hc t", p=128)), "ld"))

    wo_used = [0, 0]
    xres_mk = [None, None]
    out_mk = [None, None]
    ev_used = [None, None]
    for nb in range(4):
        wsl = nb % 2
        if nb >= 2:
            em.wait("gpsimd", "te", wo_used[wsl])
            em.wait("sync", "te", wo_used[wsl])
            wo_mk[wsl] = em.op("gpsimd", lambda e, nb=nb, wsl=wsl: e.dma_start(
                out=wo_nb[wsl].ap().rearrange("p (hc od) -> p hc od", hc=16),
                in_=wout_in[:, nb * 512:(nb + 1) * 512].rearrange(
                    "(hc p) od -> p hc od", p=128)), "gd")
            fo_mk[wsl] = em.op("sync", lambda e, nb=nb, wsl=wsl: e.dma_start(
                out=fo_nb[wsl].ap().rearrange("p (dc od) -> p dc od", dc=16),
                in_=agF_out[:, nb * 512:(nb + 1) * 512].rearrange(
                    "(dc p) od -> p dc od", p=128)), "ld")
        for tt in range(8):
            psl = tt % 2
            pbank = ps_v if psl == 0 else ps_bc
            em.wait_mark("sync", ev_used[psl])
            xres_mk[psl] = em.op("sync", lambda e, tt=tt: e.dma_start(
                out=xres[tt % 2][:, :],
                in_=x_in.rearrange("(g r) d -> g r d", g=32)[
                    _p4(e)["p8"] + tt]), "ld")
            em.wait_mark("tensor", gath_mk[-1])
            em.wait_mark("tensor", mk_xno)
            em.wait_mark("tensor", wo_mk[wsl])
            em.wait_mark("tensor", fo_mk[wsl])
            em.wait_mark("tensor", ev_used[psl])
            first = True
            for hh in range(16):
                em.op("tensor", lambda e, hh=hh, tt=tt, wsl=wsl, pbank=pbank,
                      first=first: e.matmul(
                          pbank[:, :],
                          gath[hh // 4][:, (hh % 4) * 1024 + tt * 128:
                                        (hh % 4) * 1024 + (tt + 1) * 128],
                          wo_nb[wsl][:, hh * 512:(hh + 1) * 512],
                          start=first, stop=False), None)
                first = False
            for dc in range(16):
                em.op("tensor", lambda e, dc=dc, tt=tt, wsl=wsl, pbank=pbank: e.matmul(
                    pbank[:, :],
                    xno[:, dc * 1024 + tt * 128: dc * 1024 + (tt + 1) * 128],
                    fo_nb[wsl][:, dc * 512:(dc + 1) * 512],
                    start=False, stop=False), None)
            mk_mm = em.op("tensor", lambda e, nb=nb, pbank=pbank: e.matmul(
                pbank[:, :], ones_row_bf[:, :], bias2_bf[0:1, nb * 512:(nb + 1) * 512],
                start=False, stop=True), "te")
            wo_used[wsl] = em.cnt["te"]
            em.wait_mark("vector", mk_mm)
            em.wait_mark("vector", xres_mk[psl])
            em.wait_mark("vector", out_mk[psl])
            mk_ev = em.op("vector", lambda e, nb=nb, psl=psl, pbank=pbank:
                          e.tensor_add(out_sb[psl][:, :], pbank[:, :],
                                       xres[psl][:, nb * 512:(nb + 1) * 512]), "ve")
            ev_used[psl] = mk_ev
            em.wait_mark("sync", mk_ev)
            out_mk[psl] = em.op("sync", lambda e, nb=nb, tt=tt, psl=psl: e.dma_start(
                out=y_out[tt * 128:(tt + 1) * 128, nb * 512:(nb + 1) * 512],
                in_=out_sb[psl][:, :]), "st")

    em.wait("sync", "st")
    em.wait("sync", "ld")
    em.wait("sync", "gd")
    em.wait("sync", "te")
    em.wait("sync", "ve")
    em.wait("sync", "sc")
    em.wait("sync", "gp")
    em.wait("sync", "cc")

    with nc.Block() as block:
        em.emit(block)
    ctx3.close()
    ctx.close()
    return nc


_NC_CACHE = {}


def _get_nc(debug=False):
    key = bool(debug)
    if key not in _NC_CACHE:
        _NC_CACHE[key] = build_nc(debug=key)
    return _NC_CACHE[key]


def _perm_rows_for_head(g):
    base = 128 * g
    return [base + d for d in range(0, 128, 2)] + [base + d for d in range(1, 128, 2)]


def make_in_maps(L0_attn_x, L0_attn_w_qkv, L0_attn_b_qkv, L0_attn_w_out,
                 L0_attn_b_out, L0_ffn_factors, L0_ffn_bias):
    x = np.asarray(L0_attn_x, np.float32)
    w_qkv = np.asarray(L0_attn_w_qkv, np.float32)
    b_qkv = np.asarray(L0_attn_b_qkv, np.float32)
    w_out = np.asarray(L0_attn_w_out, np.float32)
    b_out = np.asarray(L0_attn_b_out, np.float32)
    factors = np.asarray(L0_ffn_factors, np.float32)
    ffn_bias = np.asarray(L0_ffn_bias, np.float32)

    woutT = np.ascontiguousarray(w_out.T)
    in_maps = []
    for c in range(8):
        p, b = c % 4, c // 4
        heads = [4 * p + i for i in range(HPC)]
        q_sel = sum((_perm_rows_for_head(g) for g in heads), [])
        k_sel = [2048 + r for r in q_sel]
        v_sel = sum(([4096 + 128 * g + d for d in range(128)] for g in heads), [])
        sel = q_sel + k_sel + v_sel
        wT = w_qkv[sel, :].T
        packed = np.ascontiguousarray(
            wT.reshape(16, 128, 1536).transpose(1, 0, 2).reshape(128, 16 * 1536))
        bsel = b_qkv[sel]
        bias_col = np.ascontiguousarray(bsel.reshape(12, 128).T)
        bqk = bsel[:1024].reshape(8, 128)
        bias_swap = np.ascontiguousarray(
            np.concatenate([bqk[:, 64:], bqk[:, :64]], axis=1).T)
        bias_vrow = np.ascontiguousarray(bsel[1024:1536].reshape(1, 512))
        in_maps.append({
            "x_in": np.ascontiguousarray(x[b]),
            "wqkv_in": packed,
            "bias_col_in": bias_col,
            "bias_swap_in": bias_swap,
            "bias_vrow_in": bias_vrow,
            "wout_in": woutT,
            "fac_in": np.ascontiguousarray(factors[:, 256 * c:256 * (c + 1), :]),
            "bout_in": np.ascontiguousarray(b_out.reshape(1, D)),
            "fbias_in": np.ascontiguousarray(ffn_bias.reshape(1, D)),
        })
    return in_maps


def kernel(L0_attn_x, L0_attn_w_qkv, L0_attn_b_qkv, L0_attn_w_out, L0_attn_b_out,
           L0_ffn_factors, L0_ffn_bias, _debug=False):
    in_maps = make_in_maps(L0_attn_x, L0_attn_w_qkv, L0_attn_b_qkv, L0_attn_w_out,
                           L0_attn_b_out, L0_ffn_factors, L0_ffn_bias)
    nc = _get_nc(debug=_debug)
    res = run_bass_kernel_spmd(nc, in_maps, core_ids=list(range(8)))
    out = np.empty((B, T, D), np.float32)
    for c in range(8):
        p, b = c % 4, c // 4
        out[b, 1024 * p:1024 * (p + 1), :] = res.results[c]["y_out"]
    if _debug:
        return out, res
    return out
